# version 01 (final)
# baseline (speedup 1.0000x reference)
import numpy as np
import ml_dtypes
import concourse.bass as bass
import concourse.mybir as mybir
from concourse.bass_utils import run_bass_kernel_spmd
from concourse.library_config import mlp as LIB_MLP
from concourse.expressions import RuntimeValue

F32 = mybir.dt.float32
BF16 = mybir.dt.bfloat16
F16 = mybir.dt.float16
I16 = mybir.dt.int16
I32 = mybir.dt.int32
U8 = mybir.dt.uint8
AF = mybir.ActivationFunctionType
ALU = mybir.AluOpType
AXX = mybir.AxisListType.X

NC_ = 8
N = 8192
D = 1024
H = 16
DH = 64
NW = 64
FF = 4096
E = 8
CAP = 2560
EPS = 1e-5
CAST_ROUNDS = True
DBG_TAPS = ()
MAX_PHASE = 9
MOE_NOGATHER = False


def host_consts():
    c = {}
    inv_freq = 1.0 / 10000.0 ** (np.arange(0, DH, 2, dtype=np.float64) / DH)
    fh = np.concatenate([inv_freq, inv_freq])
    t = np.arange(N, dtype=np.float64)
    sgn = np.where(np.arange(DH) < 32, -1.0, 1.0)
    c["cosq"] = np.cos((t + 128)[:, None] * fh[None, :]).astype(np.float16)
    c["sinq"] = (np.sin((t + 128)[:, None] * fh[None, :]) * sgn).astype(np.float16)
    c["cosk"] = np.cos(t[:, None] * fh[None, :]).astype(np.float16)
    c["sink"] = (np.sin(t[:, None] * fh[None, :]) * sgn).astype(np.float16)
    c["identh"] = np.eye(128, dtype=np.float16)
    c["identb"] = np.eye(128, dtype=ml_dtypes.bfloat16)
    c["lt128"] = np.triu(np.ones((128, 128), dtype=np.float32), 1)
    lta = np.zeros((65, 64), dtype=np.float32)
    lta[:64, :] = np.triu(np.ones((64, 64), dtype=np.float32), 1)
    lta[64, :] = 1.0
    c["lt64aug"] = lta
    bo = np.zeros((128, 8), dtype=np.float32)
    for b in range(8):
        bo[16 * b:16 * (b + 1), b] = 1.0
    c["blockones"] = bo
    c["iotae"] = np.broadcast_to(np.arange(8, dtype=np.float32), (128, 8)).copy()
    ii = np.arange(64)[None, :].repeat(128, 0)
    pp = np.arange(128)[:, None].repeat(64, 1)
    c["tloc"] = (ii * 128 + pp).astype(np.float32)
    c["iota16"] = np.broadcast_to(np.arange(16, dtype=np.float32), (128, 16)).copy()
    c["iota1280"] = np.broadcast_to(np.arange(1280, dtype=np.float32), (128, 1280)).copy()
    cc_ = np.arange(160)[None, :].repeat(128, 0)
    sp_ = (np.arange(128) % 16)[:, None].repeat(160, 1)
    c["posstr"] = (16 * cc_ + sp_).astype(np.float32)
    frc = np.where((sp_ == 0) & (cc_ % 32 == 0), 0.0, -1.0).astype(np.float32)
    c["force"] = frc
    c["iota512"] = np.broadcast_to(np.arange(512, dtype=np.float32), (128, 512)).copy()
    c["kbase128"] = np.broadcast_to(np.arange(0, CAP, 512, dtype=np.float32), (128, 5)).copy()
    c["kbase"] = np.arange(0, CAP, 512, dtype=np.float32)[None, :].copy()
    return c


class Emit:
    def __init__(self, active, eng, sems):
        self.active = active
        self.eng = eng
        self.sems = sems
        self.S = {k: 0 for k in sems}

    def op(self, tag, fn, inc=None, waits=None, amt=1):
        if self.active == tag:
            if waits:
                for ch, val in waits:
                    if val > 0:
                        self.eng.wait_ge(self.sems[ch], val)
            inst = fn(self.eng)
            if inc is not None and inst is not None:
                inst.then_inc(self.sems[inc], amt)
        if inc is not None:
            self.S[inc] += amt

    def waits(self, tag, waits):
        if self.active == tag:
            for ch, val in waits:
                if val > 0:
                    self.eng.wait_ge(self.sems[ch], val)

    def barrier(self):
        w = [(ch, self.S[ch]) for ch in ("te", "dv", "ac", "gp", "sy", "gd", "cc")]
        for tag in ("te", "dv", "ac", "gp", "sy"):
            self.waits(tag, w)


def ap(x):
    return x.ap() if hasattr(x, "ap") else x


def sbuf_map(nc):
    M = {}
    ZONE = (nc.sbuf_top - nc.sbuf_base) - 4096
    zone = nc.alloc_sbuf_tensor("manualzone", [128, ZONE], mybir.dt.uint8)
    BASE = (nc.lookup_mloc(zone).addr + 63) // 64 * 64
    off = [BASE]

    def res(name, shape, dt):
        o = (off[0] + 31) // 32 * 32
        M[name] = nc.alloc_sbuf_tensor_at(name, list(shape), dt, offset=o)
        off[0] = o + int(np.prod(shape[1:])) * mybir.dt.size(dt)

    res("identh", [128, 128], F16)
    res("identb", [128, 128], BF16)
    res("lt128", [128, 128], F32)
    res("lt64aug", [65, 64], F32)
    res("blockones", [128, 8], F32)
    res("iotae", [128, 8], F32)
    res("tloc", [128, 64], F32)
    res("iota16", [128, 16], F32)
    res("posstr", [128, 160], F32)
    res("force", [128, 160], F32)
    res("kbase128", [128, 5], F32)
    res("kbase", [1, 8], F32)
    res("esel", [1, 32], F32)
    res("qs128", [128, DH], F32)
    res("ks128", [128, DH], F32)
    res("gb128", [128, E], F32)
    res("gwT_sb", [128, 8, E], BF16)
    res("cconst", [128, 8], F32)
    res("bmask", [8, 4], F32)
    res("logown", [128, 64, E], F32)
    res("scl2", [128, 64, 2], F32)
    res("eown", [128, 64, 2], F32)
    res("gr2", [128, 64, 2], F32)
    res("keep2", [128, 64, 2], F32)
    res("pown", [128, 64, 2], F32)
    res("yri0", [128, 64], I32)
    res("yri1", [128, 64], I32)
    res("xidx00", [128, 160], I16)
    res("xidx01", [128, 160], I16)
    res("xidx10", [128, 160], I16)
    res("xidx11", [128, 160], I16)
    res("counts", [8, 16], F32)
    res("off128", [128, 8], F32)
    res("offsh", [128, 16], F32)
    res("cnti32", [1, 32], I32)
    res("rsml", [1, 128], F32)
    res("mn", [128, 8], F32)
    RES_END = (off[0] + 511) // 512 * 512
    LIMIT = BASE + ZONE - 64

    def era(name, shape, dt, o):
        o2 = (RES_END + o + 31) // 32 * 32
        M[name] = nc.alloc_sbuf_tensor_at(name, list(shape), dt, offset=o2)
        per = int(np.prod(shape[1:])) * mybir.dt.size(dt)
        if o2 + per > LIMIT:
            raise RuntimeError(f"sbuf overflow {name} end={o2 + per}")
        return o2 + per - RES_END

    o = 0
    for spec in (("wqkv_sb", [128, 8, 3 * D], BF16), ("wout_sb", [128, 8, D], BF16),
                 ("pTr0", [128, H, 384], BF16), ("pTr1", [128, H, 384], BF16),
                 ("pTr2", [128, H, 384], BF16),
                 ("qTr0", [64, H, 128], F16), ("qTr1", [64, H, 128], F16),
                 ("qTr2", [64, H, 128], F16),
                 ("kTr0", [64, H, 128], F16), ("kTr1", [64, H, 128], F16),
                 ("vr0", [128, H, 65], BF16), ("vr1", [128, H, 65], BF16),
                 ("vr2", [128, H, 65], BF16), ("vr3", [128, H, 65], BF16),
                 ("xf32_0", [128, D], F32), ("xf32_1", [128, D], F32),
                 ("xbf_0", [128, D], BF16), ("xbf_1", [128, D], BF16),
                 ("xT", [128, 8, 128], BF16),
                 ("qtmp", [128, D], F32), ("ktmp", [128, D], F32),
                 ("q2", [128, D], F32), ("scrA", [128, D], F32),
                 ("y1", [128, D], F32), ("hwin", [128, D], F32),
                 ("xres", [128, D], F32),
                 ("t1", [128, D], F16), ("t2", [128, D], F16), ("qhat", [128, D], F16),
                 ("cost", [128, DH], F16), ("sint", [128, DH], F16),
                 ("ssq", [128, H], F32), ("rcp", [128, H], F32), ("r8", [128, H], F32),
                 ("rs", [128, H], F32),
                 ("attnout", [128, D], BF16), ("attnT", [128, 8, 128], BF16),
                 ("hT", [128, 8, 128], BF16), ("hbf", [128, D], BF16),
                 ("w1stage", [128, FF // 2], BF16)):
        o = era(spec[0], spec[1], spec[2], o)

    o = 0
    for spec in (("lgt", [128, 512, E], F32),
                 ("ohg", [128, 512, 2, E], F32), ("rw2", [128, 512, E], F32),
                 ("rw1", [128, 512], F32), ("probown", [128, 64, E], F32),
                 ("ohown", [128, 64, 2, E], F32), ("ltp", [128, 64, 16], F32),
                 ("ipb", [128, 64, 16], F32), ("tots", [65, 16], F32),
                 ("rwa", [128, 128], F32), ("rwb", [128, 128], F32),
                 ("rwi", [128, 128], I32),
                 ("abuf", [128, 128], F32), ("vklo", [128, 128], F32),
                 ("vkhi", [128, 128], F32), ("iota1280", [128, 1280], F32),
                 ("iseq", [128, 16], F32), ("volo", [128, 2, 32], F16),
                 ("oha", [128, 2, 1280], F16), ("inv16", [32, 1280], F16),
                 ("strlo", [128, 160], F16), ("strhi", [128, 160], F16),
                 ("tstr", [128, 160], F32), ("mstr", [128, 160], F32),
                 ("wstr", [128, 160], F32), ("vstr", [128, 160], F32),
                 ):
        o = era(spec[0], spec[1], spec[2], o)

    o = 0
    for spec in (("w2_sb", [128, 32, D], BF16), ("hmid", [128, 32, 512], BF16),
                 ("xbA", [128, 8, 512], BF16), ("xbB", [128, 8, 512], BF16),
                 ("xbm", [128, 8, 512], BF16),
                 ("w1t0", [128, 8, 512], BF16), ("w1t1", [128, 8, 512], BF16),
                 ("iota512", [128, 512], F32), ("selc", [128, 512], U8),
                 ("b1_sb", [128, 32], F32), ("b2_128", [128, D], F32),
                 ("ybsb", [128, D], BF16), ("zrow", [1, D], BF16)):
        o = era(spec[0], spec[1], spec[2], o)

    o = 0
    for spec in (("gbuf0", [128, 4, D], BF16), ("gbuf1", [128, 4, D], BF16),
                 ("hbuf", [128, 4, D], F32), ("ybuf", [128, D], F32),
                 ("scr4", [128, D], F32)):
        o = era(spec[0], spec[1], spec[2], o)
    return M


def emit_program(em, dbg, DR, SB):
    S = em.S
    te, dv, ac, gp, sy, gd, cc = "te", "dv", "ac", "gp", "sy", "gd", "cc"
    O = em.op
    IC = DR["IC"]
    st = dict(
        xf32_rel=[0, 0], xbf_rel=[0, 0], pqkv_rel=[0, 0], ptr_rel=[0, 0],
        trig_rel=0, psim_rel=0, ppo_rel=0, pat_rel=[0, 0], pwout_rel=0,
        plog_rel=0, xres_rel=0, xT_lastuse=0,
        qT_lastuse=[0, 0, 0], kT_lastuse=[0, 0], v_lastuse=[0, 0, 0, 0],
        pT_lastuse=[0, 0, 0], qT_ready=[0, 0, 0], kT_ready=[0, 0],
        v_ready=[0, 0, 0, 0], pT_ready_gp=[0, 0, 0], pT_ready_ac=[0, 0, 0],
        ereg_sy=None, hwin_rel=0, hbf_rel=0, qhat_rel=0, attnout_rel=0,
    )

    O(gp, lambda e: e.dma_start(out=SB["wqkv_sb"][:, :, :],
                                in_=DR["wqkv"].rearrange("(k p) o -> p k o", p=128)), inc=gd, amt=16)
    O(gp, lambda e: e.dma_start(out=SB["wout_sb"][:, :, :],
                                in_=DR["wout"].rearrange("(k p) o -> p k o", p=128)), inc=gd, amt=16)
    O(gp, lambda e: e.dma_start(out=SB["gwT_sb"][:, :, :],
                                in_=DR["gwT"].rearrange("(k p) o -> p k o", p=128)), inc=gd, amt=16)
    for i in range(8):
        for hf in range(2):
            O(gp, lambda e, i=i, hf=hf: e.dma_start(
                out=SB["w1stage"][:, :],
                in_=DR["w1"][128 * i:128 * (i + 1), 2048 * hf:2048 * (hf + 1)]),
              inc=gd, amt=16, waits=([(gd, S[gd])] if (i or hf) else None))
            O(gp, lambda e, i=i, hf=hf: e.dma_start(
                out=DR["w1bf"][128 * i:128 * (i + 1), 2048 * hf:2048 * (hf + 1)],
                in_=SB["w1stage"][:, :]), inc=gd, amt=16,
              waits=[(gd, S[gd])])
    O(sy, lambda e: e.dma_start(out=SB["qs128"][:, :], in_=DR["qs"].broadcast_to([128, DH])), inc=sy, amt=16)
    O(sy, lambda e: e.dma_start(out=SB["ks128"][:, :], in_=DR["ks"].broadcast_to([128, DH])), inc=sy, amt=16)
    O(sy, lambda e: e.dma_start(out=SB["gb128"][:, :], in_=DR["gb"].broadcast_to([128, E])), inc=sy, amt=16)
    for nm in ("identh", "identb", "lt128", "lt64aug", "blockones",
               "iotae", "tloc", "iota16", "posstr", "force", "kbase128"):
        O(sy, lambda e, nm=nm: e.dma_start(out=SB[nm][...], in_=ap(IC[nm])[...]), inc=sy, amt=16)
    O(sy, lambda e: e.dma_start(out=SB["kbase"][:, 0:5], in_=ap(IC["kbase"])[:, :]), inc=sy, amt=16)
    O(sy, lambda e: e.dma_start(out=SB["cconst"][:, :], in_=DR["cconst"][:, :]), inc=sy, amt=16)
    O(sy, lambda e: e.dma_start(out=SB["bmask"][:, :], in_=DR["bmask"][:, :]), inc=sy, amt=16)
    O(sy, lambda e: e.dma_start(out=SB["esel"][:, :], in_=DR["esel"][:, :]), inc=sy, amt=16)
    consts_sy = S[sy]
    consts_gd = S[gd]

    def cast_x(s):
        for j in range(8):
            r0 = 1024 * s + 128 * j
            sl = j % 2
            O(sy, lambda e, r0=r0, sl=sl: e.dma_start(out=SB[f"xf32_{sl}"][:, :],
                                                      in_=DR["x"][r0:r0 + 128, :]),
              inc=sy, amt=16, waits=[(dv, st["xf32_rel"][sl])])
            O(dv, lambda e, sl=sl: e.tensor_copy(out=SB[f"xbf_{sl}"][:, :],
                                                 in_=SB[f"xf32_{sl}"][:, :]),
              inc=dv, waits=[(sy, S[sy]), (gd, st["xbf_rel"][sl])])
            st["xf32_rel"][sl] = S[dv]
            O(gp, lambda e, r0=r0, sl=sl: e.dma_start(out=DR["xbf"][r0:r0 + 128, :],
                                                      in_=SB[f"xbf_{sl}"][:, :]),
              inc=gd, amt=16, waits=[(dv, S[dv])])
            st["xbf_rel"][sl] = S[gd]

    def produce(w):
        r0 = 128 * w
        qslot, kslot, vslot = w % 3, w % 2, w % 4
        for k in range(8):
            O(sy, lambda e, k=k, r0=r0: e.dma_start_transpose(
                out=SB["xT"][:, k, :], in_=DR["xbf"][r0:r0 + 128, 128 * k:128 * (k + 1)]),
              inc=sy, amt=16,
              waits=([(gd, st["xbf_done"]), (te, st["xT_lastuse"])] if k == 0 else None))
        xT_ld = S[sy]
        for sl6 in range(6):
            pn = "PA" if sl6 % 2 == 0 else "PB"
            for k in range(8):
                O(te, lambda e, k=k, sl6=sl6, pn=pn: e.matmul(
                    out=SB[pn][:, :], lhsT=SB["xT"][:, k, :],
                    rhs=SB["wqkv_sb"][:, k, 512 * sl6:512 * (sl6 + 1)],
                    start=(k == 0), stop=(k == 7)),
                    inc=te,
                    waits=([(sy, xT_ld), (gd, consts_gd),
                            (dv, st["pqkv_rel"][sl6 % 2])] if k == 0 else None))
            if sl6 < 2:
                O(dv, lambda e, sl6=sl6, pn=pn: e.tensor_copy(
                    out=SB["qtmp"][:, 512 * sl6:512 * (sl6 + 1)], in_=SB[pn][:, :]),
                    inc=dv, waits=[(te, S[te])])
            elif sl6 < 4:
                O(dv, lambda e, sl6=sl6, pn=pn: e.tensor_copy(
                    out=SB["ktmp"][:, 512 * (sl6 - 2):512 * (sl6 - 1)], in_=SB[pn][:, :]),
                    inc=dv, waits=[(te, S[te])])
            else:
                h0 = 8 * (sl6 - 4)
                O(dv, lambda e, h0=h0, pn=pn, vslot=vslot: e.tensor_copy(
                    out=SB[f"vr{vslot}"][:, h0:h0 + 8, 0:64],
                    in_=SB[pn].ap()[:, :].rearrange("p (h c) -> p h c", c=64)),
                    inc=dv, waits=[(te, S[te]), (te, st["v_lastuse"][vslot])])
            st["pqkv_rel"][sl6 % 2] = S[dv]
        O(dv, lambda e, vslot=vslot: e.memset(SB[f"vr{vslot}"][:, :, 64:65], 1.0), inc=dv)
        st["v_ready"][vslot] = S[dv]

        for nm in ("q", "k"):
            tmp = "qtmp" if nm == "q" else "ktmp"
            scl = "qs128" if nm == "q" else "ks128"
            cosn = "cosq" if nm == "q" else "cosk"
            sinn = "sinq" if nm == "q" else "sink"
            sqscale = 64.0 if nm == "q" else 1.0
            rslot = qslot if nm == "q" else kslot
            dstr = ("qTr" if nm == "q" else "kTr") + str(rslot)
            lastuse = st["qT_lastuse"] if nm == "q" else st["kT_lastuse"]
            ready = st["qT_ready"] if nm == "q" else st["kT_ready"]
            O(ac, lambda e, tmp=tmp: e.activation(out=SB["scrA"][:, :], in_=SB[tmp][:, :],
                                                  func=AF.Square),
              inc=ac, waits=[(dv, S[dv])])
            O(dv, lambda e: e.reduce_sum(out=SB["ssq"][:, :],
                                         in_=SB["scrA"].ap().rearrange("p (h c) -> p h c", c=64),
                                         axis=AXX), inc=dv, waits=[(ac, S[ac])])
            O(dv, lambda e: e.tensor_scalar(out=SB["ssq"][:, :], in0=SB["ssq"][:, :],
                                            scalar1=1e-24, scalar2=None, op0=ALU.add), inc=dv,
              waits=[(dv, S[dv])])
            O(dv, lambda e: e.reciprocal(out=SB["rcp"][:, :], in_=SB["ssq"][:, :]), inc=dv,
              waits=[(dv, S[dv])])
            O(ac, lambda e, sqscale=sqscale: e.activation(out=SB["r8"][:, :], in_=SB["rcp"][:, :],
                                                          func=AF.Sqrt, scale=sqscale),
              inc=ac, waits=[(dv, S[dv])])
            O(dv, lambda e, tmp=tmp, scl=scl: e.tensor_tensor(
                out=SB["q2"].ap().rearrange("p (h c) -> p h c", c=64),
                in0=SB[tmp].ap().rearrange("p (h c) -> p h c", c=64),
                in1=SB[scl].ap().unsqueeze(1).broadcast_to([128, H, 64]),
                op=ALU.mult), inc=dv, waits=[(sy, consts_sy)])
            O(sy, lambda e, cosn=cosn, r0=r0: e.dma_start(out=SB["cost"][:, :],
                                                          in_=ap(IC[cosn])[r0:r0 + 128, :]),
              inc=sy, amt=16, waits=[(dv, st["trig_rel"])])
            O(sy, lambda e, sinn=sinn, r0=r0: e.dma_start(out=SB["sint"][:, :],
                                                          in_=ap(IC[sinn])[r0:r0 + 128, :]),
              inc=sy, amt=16)
            trig_ld = S[sy]
            O(dv, lambda e: e.tensor_tensor(
                out=SB["t1"].ap().rearrange("p (h c) -> p h c", c=64),
                in0=SB["q2"].ap().rearrange("p (h c) -> p h c", c=64),
                in1=SB["cost"].ap().unsqueeze(1).broadcast_to([128, H, 64]),
                op=ALU.mult), inc=dv, waits=[(sy, trig_ld)])
            O(dv, lambda e: e.tensor_tensor(
                out=SB["t2"].ap().rearrange("p (h w c) -> p h w c", w=2, c=32),
                in0=SB["q2"].ap().rearrange("p (h w c) -> p h w c", w=2, c=32)[:, :, ::-1, :],
                in1=SB["sint"].ap().rearrange("p (w c) -> p w c", w=2).unsqueeze(1).broadcast_to(
                    [128, H, 2, 32]),
                op=ALU.mult), inc=dv)
            O(dv, lambda e: e.tensor_add(out=SB["t1"][:, :], in0=SB["t1"][:, :],
                                         in1=SB["t2"][:, :]), inc=dv)
            st["trig_rel"] = S[dv]
            O(dv, lambda e: e.tensor_tensor(
                out=SB["qhat"].ap().rearrange("p (h c) -> p h c", c=64),
                in0=SB["t1"].ap().rearrange("p (h c) -> p h c", c=64),
                in1=SB["r8"].ap().unsqueeze(2).broadcast_to([128, H, 64]),
                op=ALU.mult), inc=dv, waits=[(ac, S[ac]), (te, st["qhat_rel"])])
            prep_dv = S[dv]
            tapn = ("qhat0" if nm == "q" else "khat0")
            if w == 0 and tapn in dbg:
                O(gp, lambda e, tapn=tapn: e.dma_start(out=dbg[tapn][:, :], in_=SB["qhat"][:, :]),
                  inc=gd, amt=16, waits=[(dv, S[dv])])
            if w == 0 and nm == "q" and "qtmp0" in dbg:
                O(gp, lambda e: e.dma_start(out=dbg["qtmp0"][:, :], in_=SB["qtmp"][:, :]),
                  inc=gd, amt=16, waits=[(dv, S[dv])])
            for grp in range(4):
                pn = "PA" if grp % 2 == 0 else "PB"
                for hh in range(4):
                    hidx = 4 * grp + hh
                    O(te, lambda e, hidx=hidx, hh=hh, pn=pn: e.transpose(
                        out=SB[pn].ap().bitcast(F16)[0:64, 256 * hh:256 * hh + 128],
                        in_=SB["qhat"][:, 64 * hidx:64 * (hidx + 1)],
                        identity=SB["identh"][:, :]),
                        inc=te,
                        waits=([(dv, prep_dv), (dv, st["ptr_rel"][grp % 2])] if hh == 0 else None))
                O(dv, lambda e, grp=grp, pn=pn, dstr=dstr: e.tensor_copy(
                    out=SB[dstr].ap()[:, 4 * grp:4 * (grp + 1), :],
                    in_=SB[pn].ap().bitcast(F16)[0:64, :].rearrange(
                        "p (a b) -> p a b", a=4)[:, :, 0:128]),
                    inc=dv, waits=[(te, S[te]), (te, lastuse[rslot])])
                st["ptr_rel"][grp % 2] = S[dv]
            st["qhat_rel"] = S[te]
            ready[rslot] = S[dv]

    def scores(m):
        mslot = m % 3
        lo, hi = max(m - 1, 0), min(m + 1, NW - 1)
        kwait = st["kT_ready"][m % 2]
        qwait = max(st["qT_ready"][w % 3] for w in range(lo, hi + 1))
        for g in range(4):
            for hh in range(4):
                h0 = 4 * g + hh
                for w in range(lo, hi + 1):
                    blk = w - m + 1
                    wts = None
                    if hh == 0 and w == lo:
                        wts = [(dv, max(kwait, qwait)), (ac, st["psim_rel"])]
                        if g == 0:
                            wts.append((te, 0))
                    O(te, lambda e, h0=h0, hh=hh, w=w, blk=blk, m=m: e.matmul(
                        out=SB["PSIM"][:, hh, 128 * blk:128 * (blk + 1)],
                        lhsT=SB[f"kTr{m % 2}"][:, h0, :],
                        rhs=SB[f"qTr{w % 3}"][:, h0, :],
                        start=True, stop=True),
                        inc=te, waits=wts)
            b0 = 128 * (lo - m + 1)
            b1 = 128 * (hi - m + 2)
            O(ac, lambda e, g=g, mslot=mslot, b0=b0, b1=b1: e.activation(
                out=SB[f"pTr{mslot}"][:, 4 * g:4 * (g + 1), b0:b1],
                in_=SB["PSIM"][:, :, b0:b1],
                func=AF.Exp), inc=ac,
                waits=[(te, S[te]), (te, st["pT_lastuse"][mslot])])
            st["psim_rel"] = S[ac]
        if m > 0:
            O(gp, lambda e, mslot=mslot: e.affine_select(
                out=SB[f"pTr{mslot}"][:, :, 0:128], in_=SB[f"pTr{mslot}"][:, :, 0:128],
                pattern=[[0, H], [1, 128]], compare_op=ALU.is_ge, fill=0.0,
                base=0, channel_multiplier=-1), inc=gp, waits=[(ac, S[ac])])
        if m < NW - 1:
            O(gp, lambda e, mslot=mslot: e.affine_select(
                out=SB[f"pTr{mslot}"][:, :, 256:384], in_=SB[f"pTr{mslot}"][:, :, 256:384],
                pattern=[[0, H], [-1, 128]], compare_op=ALU.is_ge, fill=0.0,
                base=0, channel_multiplier=1), inc=gp, waits=[(ac, S[ac])])
        st["pT_ready_gp"][mslot] = S[gp]
        st["pT_ready_ac"][mslot] = S[ac]
        if m == 0 and "ptr0" in dbg:
            O(gp, lambda e: e.dma_start(out=dbg["ptr0"][:, :, :], in_=SB["pTr0"][:, :, :]),
              inc=gd, amt=16, waits=[(gp, S[gp]), (ac, S[ac])])
        for w in range(lo, hi + 1):
            st["qT_lastuse"][w % 3] = S[te]
        st["kT_lastuse"][m % 2] = S[te]

    def consume(i):
        r0 = 128 * i
        lo, hi = max(i - 1, 0), min(i + 1, NW - 1)
        pw_gp = max(st["pT_ready_gp"][m % 3] for m in range(lo, hi + 1))
        pw_ac = max(st["pT_ready_ac"][m % 3] for m in range(lo, hi + 1))
        vw = max(st["v_ready"][m % 4] for m in range(lo, hi + 1))
        for h0 in range(H):
            bank, sl7 = divmod(h0, 7)
            for m in range(lo, hi + 1):
                blk = i - m + 1
                O(te, lambda e, h0=h0, m=m, blk=blk, bank=bank, sl7=sl7, lo=lo, hi=hi: e.matmul(
                    out=SB["PO"][:, 512 * bank + 65 * sl7:512 * bank + 65 * sl7 + 65],
                    lhsT=SB[f"pTr{m % 3}"][:, h0, 128 * blk:128 * (blk + 1)],
                    rhs=SB[f"vr{m % 4}"][:, h0, :],
                    start=(m == lo), stop=(m == hi)),
                    inc=te,
                    waits=([(gp, pw_gp), (ac, pw_ac), (dv, vw),
                            (dv, st["ppo_rel"])] if (h0 == 0 and m == lo) else None))
        pv_te = S[te]
        for m in range(lo, hi + 1):
            st["v_lastuse"][m % 4] = pv_te
        st["pT_lastuse"][(i - 1) % 3] = pv_te
        if i == NW - 1:
            st["pT_lastuse"][i % 3] = pv_te
        for bank in range(3):
            nh = (7, 7, 2)[bank]
            O(dv, lambda e, bank=bank, nh=nh: e.reciprocal(
                out=SB["rs"][:, 7 * bank:7 * bank + nh],
                in_=SB["PO"].ap()[:, 512 * bank + 64:512 * bank + 64 + 65 * (nh - 1) + 1:65]),
                inc=dv, waits=[(te, pv_te)])
        for h0 in range(H):
            bank, sl7 = divmod(h0, 7)
            O(dv, lambda e, h0=h0, bank=bank, sl7=sl7: e.tensor_scalar(
                out=SB["attnout"][:, 64 * h0:64 * (h0 + 1)],
                in0=SB["PO"][:, 512 * bank + 65 * sl7:512 * bank + 65 * sl7 + 64],
                scalar1=SB["rs"][:, h0:h0 + 1], scalar2=None, op0=ALU.mult),
                inc=dv, waits=([(te, st["attnout_rel"]), (dv, S[dv])] if h0 == 0 else None))
        ao_dv = S[dv]
        for k in range(8):
            pn = "PA" if k % 2 == 0 else "PB"
            O(te, lambda e, k=k, pn=pn: e.transpose(
                out=SB[pn].ap().bitcast(BF16)[:, 0:128],
                in_=SB["attnout"][:, 128 * k:128 * (k + 1)],
                identity=SB["identb"][:, :]), inc=te,
                waits=[(dv, ao_dv), (dv, st["pat_rel"][k % 2])])
            O(dv, lambda e, k=k, pn=pn: e.tensor_copy(out=SB["attnT"][:, k, :],
                                                      in_=SB[pn].ap().bitcast(BF16)[:, 0:128]),
              inc=dv, waits=[(te, S[te])])
            st["pat_rel"][k % 2] = S[dv]
        if i == 0 and "att0" in dbg:
            O(gp, lambda e: e.dma_start(out=dbg["att0"][:, :], in_=SB["attnout"][:, :]),
              inc=gd, amt=16, waits=[(dv, at_dv if False else S[dv])])
        if i == 0 and "rs0" in dbg:
            O(gp, lambda e: e.dma_start(out=dbg["rs0"][:, :], in_=SB["rs"][:, :]),
              inc=gd, amt=16, waits=[(dv, S[dv])])
        if i == 0 and "vr0" in dbg:
            O(gp, lambda e: e.dma_start(out=dbg["vr0"][:, :, :], in_=SB["vr0"][:, :, :]),
              inc=gd, amt=16, waits=[(dv, S[dv])])
        st["attnout_rel"] = S[te]
        at_dv = S[dv]
        for sl2 in range(2):
            for k in range(8):
                O(te, lambda e, sl2=sl2, k=k: e.matmul(
                    out=SB["PO"][:, 512 * sl2:512 * (sl2 + 1)],
                    lhsT=SB["attnT"][:, k, :],
                    rhs=SB["wout_sb"][:, k, 512 * sl2:512 * (sl2 + 1)],
                    start=(k == 0), stop=(k == 7)),
                    inc=te,
                    waits=([(dv, at_dv), (dv, st["pwout_rel"])] if (k == 0 and sl2 == 0) else None))
        wo_te = S[te]
        O(sy, lambda e, r0=r0: e.dma_start(out=SB["xres"][:, :], in_=DR["x"][r0:r0 + 128, :]),
          inc=sy, amt=16, waits=[(dv, st["xres_rel"])])
        O(dv, lambda e: e.tensor_add(out=SB["y1"][:, :], in0=SB["PO"][:, 0:1024],
                                     in1=SB["xres"][:, :]),
          inc=dv, waits=[(te, wo_te), (sy, S[sy])])
        st["pwout_rel"] = S[dv]
        st["xres_rel"] = S[dv]
        if i == 0 and "y10" in dbg:
            O(gp, lambda e: e.dma_start(out=dbg["y10"][:, :], in_=SB["y1"][:, :]),
              inc=gd, amt=16, waits=[(dv, S[dv])])
        O(dv, lambda e: e.reduce_sum(out=SB["mn"][:, 0:1], in_=SB["y1"][:, :], axis=AXX), inc=dv)
        O(ac, lambda e: e.activation(out=SB["scrA"][:, :], in_=SB["y1"][:, :], func=AF.Square,
                                     accum_out=SB["mn"][:, 1:2]), inc=ac, waits=[(dv, S[dv])])
        O(dv, lambda e: e.tensor_scalar(out=SB["mn"][:, 0:1], in0=SB["mn"][:, 0:1],
                                        scalar1=1.0 / D, scalar2=None, op0=ALU.mult), inc=dv,
          waits=[(ac, S[ac])])
        O(dv, lambda e: e.tensor_tensor(out=SB["mn"][:, 2:3], in0=SB["mn"][:, 0:1],
                                        in1=SB["mn"][:, 0:1], op=ALU.mult), inc=dv,
          waits=[(dv, S[dv])])
        O(dv, lambda e: e.tensor_scalar(out=SB["mn"][:, 1:2], in0=SB["mn"][:, 1:2],
                                        scalar1=1.0 / D, scalar2=EPS, op0=ALU.mult,
                                        op1=ALU.add), inc=dv)
        O(dv, lambda e: e.tensor_sub(out=SB["mn"][:, 1:2], in0=SB["mn"][:, 1:2],
                                     in1=SB["mn"][:, 2:3]), inc=dv,
          waits=[(dv, S[dv])])
        O(dv, lambda e: e.reciprocal(out=SB["mn"][:, 3:4], in_=SB["mn"][:, 1:2]), inc=dv,
          waits=[(dv, S[dv])])
        O(ac, lambda e: e.activation(out=SB["mn"][:, 4:5], in_=SB["mn"][:, 3:4], func=AF.Sqrt),
          inc=ac, waits=[(dv, S[dv])])
        O(dv, lambda e: e.tensor_scalar(out=SB["hwin"][:, :], in0=SB["y1"][:, :],
                                        scalar1=SB["mn"][:, 0:1], scalar2=SB["mn"][:, 4:5],
                                        op0=ALU.subtract, op1=ALU.mult), inc=dv,
          waits=[(ac, S[ac]), (sy, st["hwin_rel"])])
        h_dv = S[dv]
        O(sy, lambda e, r0=r0: e.dma_start(out=DR["h32"][r0:r0 + 128, :], in_=SB["hwin"][:, :]),
          inc=sy, amt=16, waits=[(dv, h_dv)])
        st["hwin_rel"] = S[sy]
        O(dv, lambda e: e.tensor_copy(out=SB["hbf"][:, :], in_=SB["hwin"][:, :]), inc=dv,
          waits=[(sy, st["hbf_rel"])])
        hbf_dv = S[dv]
        O(sy, lambda e, r0=r0: e.dma_start(out=ap(DR["hagin"])[r0:r0 + 128, :], in_=SB["hbf"][:, :]),
          inc=sy, amt=16, waits=[(dv, hbf_dv)])
        st["hbf_rel"] = S[sy]
        for k in range(8):
            pn = "PA" if k % 2 == 0 else "PB"
            O(te, lambda e, k=k, pn=pn: e.transpose(
                out=SB[pn].ap().bitcast(BF16)[:, 0:128],
                in_=SB["hbf"][:, 128 * k:128 * (k + 1)],
                identity=SB["identb"][:, :]), inc=te,
                waits=[(dv, hbf_dv), (dv, st["pat_rel"][k % 2])])
            O(dv, lambda e, k=k, pn=pn: e.tensor_copy(out=SB["hT"][:, k, :],
                                                      in_=SB[pn].ap().bitcast(BF16)[:, 0:128]),
              inc=dv, waits=[(te, S[te])])
            st["pat_rel"][k % 2] = S[dv]
        hT_dv = S[dv]
        for k in range(8):
            O(te, lambda e, k=k: e.matmul(
                out=SB["PO"][:, 1024:1032], lhsT=SB["hT"][:, k, :], rhs=SB["gwT_sb"][:, k, :],
                start=(k == 0), stop=(k == 7)), inc=te,
                waits=([(dv, hT_dv), (dv, st["plog_rel"])] if k == 0 else None))
        O(dv, lambda e, i=i: e.tensor_add(out=SB["logown"][:, i, :], in0=SB["PO"][:, 1024:1032],
                                          in1=SB["gb128"][:, :]), inc=dv, waits=[(te, S[te])])
        st["plog_rel"] = S[dv]
        O(sy, lambda e, r0=r0, i=i: e.dma_start(out=ap(DR["lgin"])[r0:r0 + 128, :],
                                                in_=SB["logown"][:, i, :]),
          inc=sy, amt=16, waits=[(dv, S[dv])])
        st["ppo_rel"] = S[dv]

    for s in range(8):
        cast_x(s)
    st["xbf_done"] = S[gd]
    for w in range(NW):
        st["xT_lastuse"] = S[te]
        produce(w)
        if w >= 1:
            scores(w - 1)
        if w >= 2:
            consume(w - 2)
    scores(NW - 1)
    consume(NW - 2)
    consume(NW - 1)
    attn_sy = S[sy]

    if "h32" in dbg:
        O(gp, lambda e: e.dma_start(out=dbg["h32"][:, :], in_=DR["h32"][:, :]),
          inc=gd, amt=16, waits=[(sy, attn_sy)])

    if MAX_PHASE < 2:
        O(dv, lambda e: e.engine_nop(),
          waits=[(sy, S[sy]), (gd, S[gd]), (cc, S[cc]), (gp, S[gp]), (ac, S[ac]), (te, S[te])])
        return
    O(gp, lambda e: e.collective_compute(
        "AllGather", ALU.bypass, replica_groups=[list(range(NC_))],
        ins=[ap(DR["lgin"]).opt()], outs=[ap(DR["lgout"]).opt()]), inc=cc,
        waits=[(sy, attn_sy)])
    lgag_cc = S[cc]
    O(gp, lambda e: e.collective_compute(
        "AllGather", ALU.bypass, replica_groups=[list(range(NC_))],
        ins=[ap(DR["hagin"]).opt()], outs=[ap(DR["hagout"]).opt()]), inc=cc)
    hag_cc = S[cc]
    if "lgout" in dbg:
        O(gp, lambda e: e.dma_start(out=dbg["lgout"][:, :], in_=ap(DR["lgout"])[:, :]),
          inc=gd, amt=16, waits=[(cc, lgag_cc)])
    if "hagr" in dbg:
        O(gp, lambda e: e.dma_start(out=dbg["hagr"][:, :], in_=ap(DR["hagout"])[7168:10240, :]),
          inc=gd, amt=16, waits=[(cc, hag_cc)])
    if "lgin0" in dbg:
        O(gp, lambda e: e.dma_start(out=dbg["lgin0"][:, :], in_=ap(DR["lgin"])[:, :]),
          inc=gd, amt=16, waits=[(sy, attn_sy)])

    if MAX_PHASE < 3:
        O(dv, lambda e: e.engine_nop(),
          waits=[(sy, S[sy]), (gd, S[gd]), (cc, S[cc]), (gp, S[gp]), (ac, S[ac]), (te, S[te])])
        return
    em.barrier()

    O(sy, lambda e: e.dma_start(out=SB["lgt"][:, :, :],
                                in_=ap(DR["lgout"]).rearrange("(p i) e -> p i e", p=128)),
      inc=sy, amt=16, waits=[(cc, lgag_cc)])
    lg_ld = S[sy]

    def top2(srcnm, nI, ohnm, evals=None, pvals=None, probnm=None, extra=None):
        for c in range(2):
            O(dv, lambda e, nI=nI: e.reduce_max(out=SB["rw1"][:, 0:nI],
                                                in_=SB[srcnm][:, 0:nI, :], axis=AXX),
              inc=dv, waits=(extra if c == 0 else None))
            O(dv, lambda e, nI=nI: e.tensor_tensor(
                out=SB["rw2"][:, 0:nI, :], in0=SB[srcnm][:, 0:nI, :],
                in1=SB["rw1"].ap()[:, 0:nI].unsqueeze(2).broadcast_to([128, nI, 8]),
                op=ALU.is_ge), inc=dv)
            O(dv, lambda e, nI=nI: e.scalar_tensor_tensor(
                out=SB["rw2"][:, 0:nI, :], in0=SB["rw2"][:, 0:nI, :], scalar=8.0,
                in1=SB["iotae"].ap().unsqueeze(1).broadcast_to([128, nI, 8]),
                op0=ALU.mult, op1=ALU.subtract), inc=dv)
            O(dv, lambda e, nI=nI: e.reduce_max(out=SB["rw1"][:, 0:nI],
                                                in_=SB["rw2"][:, 0:nI, :], axis=AXX), inc=dv)
            O(dv, lambda e, c=c, nI=nI: e.tensor_tensor(
                out=SB[ohnm][:, 0:nI, c, :], in0=SB["rw2"][:, 0:nI, :],
                in1=SB["rw1"].ap()[:, 0:nI].unsqueeze(2).broadcast_to([128, nI, 8]),
                op=ALU.is_ge), inc=dv)
            if evals is not None:
                O(dv, lambda e, c=c, nI=nI: e.tensor_tensor(
                    out=SB["rw2"][:, 0:nI, :], in0=SB[ohnm][:, 0:nI, c, :],
                    in1=SB["iotae"].ap().unsqueeze(1).broadcast_to([128, nI, 8]),
                    op=ALU.mult), inc=dv)
                O(dv, lambda e, c=c, nI=nI: e.reduce_sum(
                    out=SB[evals][:, 0:nI, c], in_=SB["rw2"][:, 0:nI, :], axis=AXX), inc=dv)
            if pvals is not None:
                O(dv, lambda e, c=c, nI=nI: e.tensor_tensor(
                    out=SB["rw2"][:, 0:nI, :], in0=SB[ohnm][:, 0:nI, c, :],
                    in1=SB[probnm][:, 0:nI, :], op=ALU.mult), inc=dv)
                O(dv, lambda e, c=c, nI=nI: e.reduce_sum(
                    out=SB[pvals][:, 0:nI, c], in_=SB["rw2"][:, 0:nI, :], axis=AXX), inc=dv)
            if c == 0:
                O(dv, lambda e, nI=nI: e.scalar_tensor_tensor(
                    out=SB[srcnm][:, 0:nI, :], in0=SB[ohnm][:, 0:nI, 0, :], scalar=-1e30,
                    in1=SB[srcnm][:, 0:nI, :], op0=ALU.mult, op1=ALU.add), inc=dv)

    if "lgt0" in dbg:
        O(gp, lambda e: e.dma_start(out=dbg["lgt0"][...], in_=SB["lgt"][...]),
          inc=gd, amt=16, waits=[(sy, lg_ld)])
    top2("lgt", 512, "ohg", extra=[(sy, lg_ld), (gd, S[gd])])
    if "ohg0" in dbg:
        O(gp, lambda e: e.dma_start(out=dbg["ohg0"][...], in_=SB["ohg"][...]),
          inc=gd, amt=16, waits=[(dv, S[dv])])
    for c in range(2):
        for e8 in range(E):
            col = 8 * c + e8
            O(dv, lambda e, c=c, e8=e8: e.tensor_copy(out=SB["rw1"][:, :],
                                                      in_=SB["ohg"][:, :, c, e8]),
              inc=dv, waits=[(te, S[te])])
            O(te, lambda e: e.matmul(
                out=SB["PA"][0:8, :], lhsT=SB["blockones"][:, :],
                rhs=SB["rw1"][:, :], start=True, stop=True), inc=te,
                waits=[(dv, S[dv])])
            O(dv, lambda e, col=col: e.reduce_sum(out=SB["counts"][:, col:col + 1],
                                                  in_=SB["PA"][0:8, :], axis=AXX),
              inc=dv, waits=[(te, S[te])])
    cnt_dv = S[dv]

    O(dv, lambda e: e.reduce_max(out=SB["rw1"][:, 0:64], in_=SB["logown"][:, :, :], axis=AXX),
      inc=dv)
    O(dv, lambda e: e.tensor_tensor(out=SB["probown"][:, :, :], in0=SB["logown"][:, :, :],
                                    in1=SB["rw1"].ap()[:, 0:64].unsqueeze(2).broadcast_to(
                                        [128, 64, 8]), op=ALU.subtract), inc=dv)
    O(ac, lambda e: e.activation(out=SB["probown"][:, :, :], in_=SB["probown"][:, :, :],
                                 func=AF.Exp), inc=ac, waits=[(dv, S[dv])])
    O(dv, lambda e: e.reduce_sum(out=SB["rw1"][:, 0:64], in_=SB["probown"][:, :, :], axis=AXX),
      inc=dv, waits=[(ac, S[ac])])
    O(dv, lambda e: e.reciprocal(out=SB["rw1"][:, 64:128], in_=SB["rw1"][:, 0:64]), inc=dv)
    O(dv, lambda e: e.tensor_tensor(out=SB["probown"][:, :, :], in0=SB["probown"][:, :, :],
                                    in1=SB["rw1"].ap()[:, 64:128].unsqueeze(2).broadcast_to(
                                        [128, 64, 8]), op=ALU.mult), inc=dv)
    if "lgown0" in dbg:
        O(gp, lambda e: e.dma_start(out=dbg["lgown0"][...], in_=SB["logown"][...]),
          inc=gd, amt=16, waits=[(dv, S[dv])])
    top2("logown", 64, "ohown", evals="eown", pvals="pown", probnm="probown")
    if "ohown0" in dbg:
        O(gp, lambda e: e.dma_start(out=dbg["ohown0"][...], in_=SB["ohown"][...]),
          inc=gd, amt=16, waits=[(dv, S[dv])])
    if "eownX" in dbg:
        O(gp, lambda e: e.dma_start(out=dbg["eownX"][...], in_=SB["eown"][...]),
          inc=gd, amt=16, waits=[(dv, S[dv])])

    for half in range(2):
        O(te, lambda e, half=half: e.matmul(
            out=SB["PO"][:, 512 * half:512 * (half + 1)], lhsT=SB["lt128"][:, :],
            rhs=SB["ohown"].ap().rearrange("p i c e -> p (i c e)")[:, 512 * half:512 * (half + 1)],
            start=True, stop=True), inc=te, waits=[(dv, S[dv])])
    O(dv, lambda e: e.tensor_copy(out=SB["ltp"].ap().rearrange("p i q -> p (i q)"),
                                  in_=SB["PO"][:, 0:1024]), inc=dv, waits=[(te, S[te])])
    O(dv, lambda e: e.memset(SB["rw1"][:, 64:65], 1.0), inc=dv)
    for c in range(2):
        for e8 in range(E):
            col = 8 * c + e8
            O(dv, lambda e, c=c, e8=e8: e.tensor_copy(out=SB["rw1"][:, 0:64],
                                                      in_=SB["ohown"][:, :, c, e8]),
              inc=dv, waits=[(te, S[te])])
            O(te, lambda e, col=col: e.matmul(
                out=SB["PB"][0:64, col:col + 1], lhsT=SB["rw1"][:, 0:64],
                rhs=SB["rw1"][:, 64:65], start=True, stop=True), inc=te,
                waits=[(dv, S[dv])])
    O(dv, lambda e: e.tensor_copy(out=SB["tots"][0:64, :], in_=SB["PB"][0:64, 0:16]),
      inc=dv, waits=[(te, S[te])])
    O(te, lambda e: e.matmul(out=SB["PB"][0:1, 0:16], lhsT=SB["bmask"][:, 0:1],
                             rhs=SB["counts"][:, :], start=True, stop=True), inc=te,
      waits=[(dv, S[dv]), (dv, cnt_dv)])
    O(dv, lambda e: e.tensor_copy(out=SB["tots"][64:65, :], in_=SB["PB"][0:1, 0:16]),
      inc=dv, waits=[(te, S[te])])
    O(te, lambda e: e.matmul(out=SB["PA"][0:64, 0:16], lhsT=SB["lt64aug"][:, :],
                             rhs=SB["tots"][:, :], start=True, stop=True), inc=te,
      waits=[(dv, S[dv])])
    O(dv, lambda e: e.tensor_copy(out=SB["tots"][0:64, :], in_=SB["PA"][0:64, 0:16]),
      inc=dv, waits=[(te, S[te])])
    O(sy, lambda e: e.dma_start(out=DR["sc2"][:, :], in_=SB["tots"][0:64, :]), inc=sy, amt=16,
      waits=[(dv, S[dv])])
    O(sy, lambda e: e.dma_start(out=SB["ipb"][:, :, :],
                                in_=DR["sc2"].unsqueeze(0).broadcast_to([128, 64, 16])),
      inc=sy, amt=16, waits=[(sy, S[sy])])
    O(dv, lambda e: e.tensor_add(out=SB["ltp"].ap().rearrange("p i q -> p (i q)"),
                                 in0=SB["ltp"].ap().rearrange("p i q -> p (i q)"),
                                 in1=SB["ipb"].ap().rearrange("p i q -> p (i q)")), inc=dv,
      waits=[(sy, S[sy])])
    O(dv, lambda e: e.tensor_tensor(
        out=SB["rw2"].ap().rearrange("p i e -> p (i e)")[:, 0:1024],
        in0=SB["ohown"].ap().rearrange("p i c e -> p (i c e)"),
        in1=SB["ltp"].ap().rearrange("p i q -> p (i q)"), op=ALU.mult), inc=dv)
    O(dv, lambda e: e.reduce_sum(
        out=SB["gr2"].ap().rearrange("p i c -> p (i c)"),
        in_=SB["rw2"].ap().rearrange("p i e -> p (i e)")[:, 0:1024].rearrange(
            "p (x e) -> p x e", e=8),
        axis=AXX), inc=dv)
    O(dv, lambda e: e.tensor_scalar(out=SB["keep2"].ap().rearrange("p i c -> p (i c)"),
                                    in0=SB["gr2"].ap().rearrange("p i c -> p (i c)"),
                                    scalar1=float(CAP), scalar2=None, op0=ALU.is_lt), inc=dv)
    O(dv, lambda e: e.tensor_tensor(out=SB["scl2"].ap().rearrange("p i c -> p (i c)"),
                                    in0=SB["pown"].ap().rearrange("p i c -> p (i c)"),
                                    in1=SB["keep2"].ap().rearrange("p i c -> p (i c)"),
                                    op=ALU.mult), inc=dv)

    for tapn, buf in (("gr20", "gr2"), ("eown0", "eown"), ("cnts0", "counts"),
                      ("tots0", "tots"), ("keep0", "keep2"), ("pown0", "pown")):
        if tapn in dbg:
            O(gp, lambda e, tapn=tapn, buf=buf: e.dma_start(out=dbg[tapn][...], in_=SB[buf][...]),
              inc=gd, amt=16, waits=[(dv, S[dv]), (te, S[te])])
    GRF = "p i c -> p (i c)"
    O(dv, lambda e: e.tensor_scalar(out=SB["rwa"][:, 0:128], in0=SB["gr2"].ap().rearrange(GRF),
                                    scalar1=1.0 / 16.0, scalar2=-0.46875,
                                    op0=ALU.mult, op1=ALU.add), inc=dv)
    O(dv, lambda e: e.tensor_copy(out=SB["rwi"][:, 0:128], in_=SB["rwa"][:, 0:128]), inc=dv)
    O(dv, lambda e: e.tensor_copy(out=SB["rwa"][:, 0:128], in_=SB["rwi"][:, 0:128]), inc=dv)
    O(dv, lambda e: e.scalar_tensor_tensor(out=SB["rwb"][:, 0:128], in0=SB["rwa"][:, 0:128],
                                           scalar=-16.0, in1=SB["gr2"].ap().rearrange(GRF),
                                           op0=ALU.mult, op1=ALU.add), inc=dv)
    O(dv, lambda e: e.scalar_tensor_tensor(out=SB["abuf"][:, 0:128],
                                           in0=SB["eown"].ap().rearrange(GRF),
                                           scalar=160.0, in1=SB["rwa"][:, 0:128],
                                           op0=ALU.mult, op1=ALU.add), inc=dv)
    for c in range(2):
        O(dv, lambda e, c=c: e.tensor_scalar(
            out=SB["vklo"].ap()[:, c:128:2], in0=SB["tloc"][:, :],
            scalar1=SB["cconst"][:, 2:3], scalar2=None, op0=ALU.add), inc=dv)
    O(dv, lambda e: e.tensor_tensor(out=SB["vklo"][:, 0:128], in0=SB["vklo"][:, 0:128],
                                    in1=SB["keep2"].ap().rearrange(GRF), op=ALU.mult), inc=dv)
    O(dv, lambda e: e.tensor_scalar(out=SB["vkhi"][:, 0:128], in0=SB["vklo"][:, 0:128],
                                    scalar1=1.0 / 2048.0, scalar2=-0.4998779296875,
                                    op0=ALU.mult, op1=ALU.add), inc=dv)
    O(dv, lambda e: e.tensor_copy(out=SB["rwi"][:, 0:128], in_=SB["vkhi"][:, 0:128]), inc=dv)
    O(dv, lambda e: e.tensor_copy(out=SB["vkhi"][:, 0:128], in_=SB["rwi"][:, 0:128]), inc=dv)
    O(dv, lambda e: e.scalar_tensor_tensor(out=SB["vklo"][:, 0:128], in0=SB["vkhi"][:, 0:128],
                                           scalar=-2048.0, in1=SB["vklo"][:, 0:128],
                                           op0=ALU.mult, op1=ALU.add), inc=dv)
    inv_in_dv = S[dv]
    O(sy, lambda e: e.dma_start(out=SB["iota1280"][...], in_=ap(IC["iota1280"])[...]),
      inc=sy, amt=16)
    i1280_sy = S[sy]
    st["vo_rel"] = [0, 0]
    for c in range(2):
        for i in range(64):
            col = 2 * i + c
            sl = i % 2
            O(dv, lambda e, col=col: e.tensor_scalar(
                out=SB["iseq"][:, :], in0=SB["iota16"][:, :],
                scalar1=SB["rwb"][:, col:col + 1], scalar2=None, op0=ALU.is_equal),
                inc=dv, waits=([(dv, inv_in_dv)] if (c == 0 and i == 0) else None))
            O(dv, lambda e, col=col, sl=sl: e.tensor_scalar(
                out=SB["volo"][:, sl, 0:16], in0=SB["iseq"][:, :],
                scalar1=SB["vklo"][:, col:col + 1], scalar2=None, op0=ALU.mult), inc=dv,
                waits=[(te, st["vo_rel"][sl])])
            O(dv, lambda e, col=col, sl=sl: e.tensor_scalar(
                out=SB["volo"][:, sl, 16:32], in0=SB["iseq"][:, :],
                scalar1=SB["vkhi"][:, col:col + 1], scalar2=None, op0=ALU.mult), inc=dv)
            O(dv, lambda e, col=col, sl=sl: e.tensor_scalar(
                out=SB["oha"][:, sl, :], in0=SB["iota1280"][:, :],
                scalar1=SB["abuf"][:, col:col + 1], scalar2=None, op0=ALU.is_equal),
                inc=dv, waits=([(sy, i1280_sy)] if (c == 0 and i == 0) else None))
            vo_dv = S[dv]
            for s3, (n0, n1) in enumerate(((0, 512), (512, 1024), (1024, 1280))):
                O(te, lambda e, i=i, s3=s3, n0=n0, n1=n1, sl=sl: e.matmul(
                    out=SB["PSIM"].ap().rearrange("p a b -> p (a b)")[0:32, n0:n1],
                    lhsT=SB["volo"][:, sl, :], rhs=SB["oha"][:, sl, n0:n1],
                    start=(i == 0), stop=(i == 63)), inc=te,
                    waits=([(dv, vo_dv)] if s3 == 0 else None))
            st["vo_rel"][sl] = S[te]
        O(dv, lambda e: e.tensor_copy(out=SB["inv16"][:, :],
                                      in_=SB["PSIM"].ap().rearrange("p a b -> p (a b)")[0:32, 0:1280]),
          inc=dv, waits=[(te, S[te])])
        O(sy, lambda e, c=c: e.dma_start(
            out=ap(DR["rtin"])[2 * c:2 * c + 1, :].rearrange("o (sp a) -> (o sp) a", sp=16),
            in_=SB["inv16"][0:16, :]),
          inc=sy, amt=16, waits=[(dv, S[dv])])
        O(sy, lambda e, c=c: e.dma_start(
            out=ap(DR["rtin"])[2 * c + 1:2 * c + 2, :].rearrange("o (sp a) -> (o sp) a", sp=16),
            in_=SB["inv16"][16:32, :]),
          inc=sy, amt=16)
    O(gp, lambda e: e.collective_compute(
        "AllReduce", ALU.add, replica_groups=[list(range(NC_))],
        ins=[ap(DR["rtin"]).opt()], outs=[ap(DR["rtout"]).opt()]), inc=cc,
        waits=[(sy, S[sy])])
    ar_cc = S[cc]
    if "rtout" in dbg:
        O(gp, lambda e: e.dma_start(out=dbg["rtout"][:, :], in_=ap(DR["rtout"])[:, :]),
          inc=gd, amt=16, waits=[(cc, ar_cc)])

    O(dv, lambda e: e.memset(SB["rw1"][0:8, 1:2], 1.0), inc=dv)
    O(te, lambda e: e.matmul(out=SB["PB"][0:1, 0:16], lhsT=SB["rw1"][0:8, 1:2],
                             rhs=SB["counts"][:, :], start=True, stop=True), inc=te,
      waits=[(dv, S[dv])])
    O(dv, lambda e: e.tensor_scalar(out=SB["rsml"][:, 0:16], in0=SB["PB"][0:1, 0:16],
                                    scalar1=float(CAP), scalar2=None, op0=ALU.min),
      inc=dv, waits=[(te, S[te])])
    O(te, lambda e: e.matmul(out=SB["PB"][0:1, 0:16], lhsT=SB["bmask"][:, 1:2],
                             rhs=SB["counts"][:, :], start=True, stop=True), inc=te,
      waits=[(dv, S[dv])])
    O(dv, lambda e: e.tensor_copy(out=SB["rsml"][:, 16:32], in_=SB["PB"][0:1, 0:16]),
      inc=dv, waits=[(te, S[te])])
    O(dv, lambda e: e.tensor_tensor(out=SB["rsml"][:, 96:128], in0=SB["rsml"][:, 0:32],
                                    in1=SB["esel"][:, :], op=ALU.mult), inc=dv,
      waits=[(dv, S[dv])])
    for g in range(4):
        O(dv, lambda e, g=g: e.reduce_sum(out=SB["rsml"][:, 32 + g:33 + g],
                                          in_=SB["rsml"][:, 96 + 8 * g:104 + 8 * g],
                                          axis=AXX), inc=dv,
          waits=([(dv, S[dv])] if g == 0 else None))
    O(dv, lambda e: e.tensor_tensor(out=SB["rsml"][:, 34:36], in0=SB["rsml"][:, 34:36],
                                    in1=SB["rsml"][:, 32:34], op=ALU.min), inc=dv,
      waits=[(dv, S[dv])])
    O(dv, lambda e: e.tensor_copy(out=SB["cnti32"][:, 30:31], in_=SB["cconst"][0:1, 0:1]),
      inc=dv)
    O(sy, lambda e: e.dma_start(out=DR["sc3"][0:1, 0:4], in_=SB["rsml"][:, 32:36]), inc=sy,
      amt=16, waits=[(dv, S[dv])])
    O(sy, lambda e: e.dma_start(out=SB["off128"][:, 0:4],
                                in_=DR["sc3"][0:1, 0:4].broadcast_to([128, 4])), inc=sy, amt=16,
      waits=[(sy, S[sy])])
    for c in range(2):
        O(dv, lambda e, c=c: e.tensor_scalar(
            out=SB["offsh"][:, 5 * c:5 * c + 5], in0=SB["kbase128"][:, :],
            scalar1=SB["off128"][:, 2 + c:3 + c], scalar2=-1.0,
            op0=ALU.subtract, op1=ALU.mult), inc=dv, waits=[(sy, S[sy])])
    offsh_dv = S[dv]

    for c in range(2):
        def _strload(e, c=c, pl=0):
            if st.get("ereg_sy") is None:
                st["ereg_sy"] = e.value_load(SB["cnti32"][0:1, 30:31], min_val=0, max_val=7)
            src = ap(DR["rtout"])[2 * c + pl:2 * c + pl + 1, :].rearrange(
                "o (sp col) -> o sp col", sp=16)[0]
            src = src.unsqueeze(0).broadcast_to([8, 16, 1280])[:, :, bass.ts(st["ereg_sy"], 160)]
            dstn = "strlo" if pl == 0 else "strhi"
            return e.dma_start(out=SB[dstn][:, :], in_=src)
        O(sy, lambda e, c=c: _strload(e, c, 0), inc=sy, amt=16,
          waits=[(cc, ar_cc), (dv, S[dv])])
        O(sy, lambda e, c=c: _strload(e, c, 1), inc=sy, amt=16)
        str_ld = S[sy]
        O(dv, lambda e: e.scalar_tensor_tensor(out=SB["tstr"][:, :], in0=SB["strhi"][:, :],
                                               scalar=2048.0, in1=SB["strlo"][:, :],
                                               op0=ALU.mult, op1=ALU.add), inc=dv,
          waits=[(sy, str_ld)])
        O(dv, lambda e, c=c: e.tensor_scalar(out=SB["mstr"][:, :], in0=SB["posstr"][:, :],
                                             scalar1=SB["off128"][:, c:c + 1], scalar2=None,
                                             op0=ALU.is_lt), inc=dv)
        for pa in range(2):
            xnm = f"xidx{c}{pa}"
            if pa == 0:
                O(dv, lambda e: e.tensor_scalar(out=SB["wstr"][:, :], in0=SB["tstr"][:, :],
                                                scalar1=32769.0, scalar2=None,
                                                op0=ALU.is_lt), inc=dv)
                O(dv, lambda e: e.tensor_copy(out=SB["vstr"][:, :], in_=SB["tstr"][:, :]), inc=dv)
            else:
                O(dv, lambda e: e.tensor_scalar(out=SB["wstr"][:, :], in0=SB["tstr"][:, :],
                                                scalar1=32769.0, scalar2=None,
                                                op0=ALU.is_ge), inc=dv)
                O(dv, lambda e: e.tensor_scalar(out=SB["vstr"][:, :], in0=SB["tstr"][:, :],
                                                scalar1=-32768.0, scalar2=None,
                                                op0=ALU.add), inc=dv)
            O(dv, lambda e: e.tensor_tensor(out=SB["wstr"][:, :], in0=SB["wstr"][:, :],
                                            in1=SB["mstr"][:, :], op=ALU.mult), inc=dv)
            O(dv, lambda e: e.tensor_tensor(out=SB["vstr"][:, :], in0=SB["vstr"][:, :],
                                            in1=SB["wstr"][:, :], op=ALU.mult), inc=dv)
            O(dv, lambda e: e.tensor_scalar(out=SB["vstr"][:, :], in0=SB["vstr"][:, :],
                                            scalar1=-1.0, scalar2=0.0, op0=ALU.add,
                                            op1=ALU.max), inc=dv)
            O(dv, lambda e, xnm=xnm: e.tensor_copy(out=SB[xnm][:, :], in_=SB["vstr"][:, :]),
              inc=dv)
    O(dv, lambda e: e.scalar_tensor_tensor(out=SB["rwb"][:, 0:128],
                                           in0=SB["eown"].ap().rearrange(GRF),
                                           scalar=float(CAP + 1),
                                           in1=SB["gr2"].ap().rearrange(GRF),
                                           op0=ALU.mult, op1=ALU.add), inc=dv)
    O(dv, lambda e: e.tensor_tensor(out=SB["rwb"][:, 0:128], in0=SB["rwb"][:, 0:128],
                                    in1=SB["keep2"].ap().rearrange(GRF), op=ALU.mult), inc=dv)
    for c in range(2):
        O(dv, lambda e, c=c: e.tensor_copy(out=SB[f"yri{c}"][:, :],
                                           in_=SB["rwb"].ap()[:, c:128:2]), inc=dv)
    for tapn, buf in (("rsml0", "rsml"), ("counts2", "counts"),
                      ("cnti", "cnti32"), ("off1280", "off128"),
                      ("xidx00t", "xidx00"), ("xidx01t", "xidx01"),
                      ("xidx10t", "xidx10"), ("xidx11t", "xidx11"),
                      ("yri0t", "yri0"), ("yri1t", "yri1"), ("scl20", "scl2")):
        if tapn in dbg:
            O(gp, lambda e, tapn=tapn, buf=buf: e.dma_start(out=dbg[tapn][...], in_=SB[buf][...]),
              inc=gd, amt=16, waits=[(dv, S[dv]), (sy, S[sy])])
    route_done = dict(dv=S[dv], sy=S[sy], te=S[te], gp=S[gp], ac=S[ac])

    if MAX_PHASE < 4:
        O(dv, lambda e: e.engine_nop(),
          waits=[(sy, S[sy]), (gd, S[gd]), (cc, S[cc]), (gp, S[gp]), (ac, S[ac]), (te, S[te])])
        return
    em.barrier()

    O(gp, lambda e: e.load_library(LIB_MLP))
    O(gp, lambda e: e.dma_start(out=SB["w2_sb"][:, :, :],
                                in_=DR["w2"].rearrange("(k p) o -> p k o", p=128)), inc=gd, amt=16)
    O(sy, lambda e: e.dma_start(out=SB["b1_sb"][:, :],
                                in_=DR["b1"].rearrange("o (k p) -> p (o k)", p=128),
                                allow_slow_non_contiguous=True), inc=sy, amt=16)
    O(sy, lambda e: e.dma_start(out=SB["b2_128"][:, :], in_=DR["b2"].broadcast_to([128, D])),
      inc=sy, amt=16)
    O(sy, lambda e: e.dma_start(out=SB["iota512"][...], in_=ap(IC["iota512"])[...]), inc=sy, amt=16)
    O(dv, lambda e: e.memset(SB["zrow"][:, :], 0.0), inc=dv)
    O(gp, lambda e: e.dma_start(out=ap(DR["yin0"])[CAP:CAP + 1, :], in_=SB["zrow"][:, :]),
      inc=gd, amt=16, waits=[(dv, S[dv])])
    O(gp, lambda e: e.dma_start(out=ap(DR["yin1"])[CAP:CAP + 1, :], in_=SB["zrow"][:, :]),
      inc=gd, amt=16)
    moe_ld_gd = S[gd]
    moe_ld_sy = S[sy]
    st["xb_rel"] = 0
    st["hmid_rel"] = 0
    st["w1t_rel"] = [0, 0]
    st["pmm1_rel"] = [0, 0]
    st["po2_rel"] = 0
    st["ybsb_rel"] = 0
    for c in range(2):
        for kk in range(5):
            O(dv, lambda e, c=c, kk=kk: e.tensor_scalar(
                out=SB["selc"][:, :], in0=SB["iota512"][:, :],
                scalar1=SB["offsh"][:, 5 * c + kk:5 * c + kk + 1], scalar2=None,
                op0=ALU.is_lt), inc=dv,
                waits=[(sy, moe_ld_sy), (dv, st["xb_rel"])])
            selc_dv = S[dv]
            if MOE_NOGATHER:
                O(dv, lambda e: e.memset(SB["xbm"][:, :, :], 0.0), inc=dv,
                  waits=[(cc, hag_cc), (dv, st["xb_rel"])])
            else:
                for pa, (bufn, srclo, srchi) in enumerate(
                        (("xbA", 0, 32768), ("xbB", 32768, 65536))):
                    xnm = f"xidx{c}{pa}"
                    def _gat(e, bufn=bufn, srclo=srclo, srchi=srchi, xnm=xnm, kk=kk):
                        return e.dma_gather(
                            SB[bufn][:, :, :],
                            ap(DR["hagout"])[srclo:srchi, :],
                            SB[xnm][:, 32 * kk:32 * (kk + 1)],
                            512, 512, D, elem_step=D, transpose=True)
                    O(gp, _gat, inc=gd, amt=16,
                      waits=[(cc, hag_cc), (dv, st["xb_rel"])] if pa == 0 else None)
                gat_gd = S[gd]
                O(dv, lambda e: e.select(out=SB["xbm"][:, :, :],
                                         mask=SB["selc"].ap().unsqueeze(1).broadcast_to([128, 8, 512]),
                                         on_true=SB["xbA"][:, :, :], on_false=SB["xbB"][:, :, :]),
                  inc=dv, waits=[(gd, gat_gd)])
            xbm_dv = S[dv]
            st["xb_rel"] = S[dv]
            if c == 1 and kk == 0:
                for tapn, bufn in (("xbm10", "xbm"), ("xbA10", "xbA"), ("xbB10", "xbB")):
                    if tapn in dbg:
                        O(gp, lambda e, tapn=tapn, bufn=bufn: e.dma_start(
                            out=dbg[tapn][:, :, :], in_=SB[bufn][:, :, :]),
                            inc=gd, amt=16, waits=[(dv, S[dv])])
            for g4 in range(8):
                wn = f"w1t{g4 % 2}"
                O(sy, lambda e, g4=g4, wn=wn: e.dma_start(
                    out=SB[wn][:, :, :],
                    in_=DR["w1bfT"].rearrange("(k p) o -> p k o", p=128)[:, :, 512 * g4:512 * (g4 + 1)]),
                    inc=sy, amt=16, waits=[(te, st["w1t_rel"][g4 % 2])])
                w1_ld = S[sy]
                for ff in range(4):
                    f = 4 * g4 + ff
                    pn = "PA" if f % 2 == 0 else "PB"
                    for k in range(8):
                        O(te, lambda e, f=f, ff=ff, k=k, pn=pn, wn=wn: e.matmul(
                            out=SB[pn][:, :], lhsT=SB[wn][:, k, 128 * ff:128 * (ff + 1)],
                            rhs=SB["xbm"][:, k, :],
                            start=(k == 0), stop=(k == 7)), inc=te,
                            waits=([(sy, w1_ld), (dv, xbm_dv), (ac, st["pmm1_rel"][f % 2]),
                                    (gd, moe_ld_gd)] if k == 0 else None))
                    O(ac, lambda e, f=f, pn=pn: e.activation(
                        out=SB["hmid"][:, f, :], in_=SB[pn][:, :], func=AF.Relu,
                        bias=SB["b1_sb"][:, f:f + 1]), inc=ac,
                        waits=[(te, S[te]), (te, st["hmid_rel"])])
                    st["pmm1_rel"][f % 2] = S[ac]
                st["w1t_rel"][g4 % 2] = S[te]
            mm1_ac = S[ac]
            for tb in range(4):
                for f in range(32):
                    for s2 in range(2):
                        O(te, lambda e, tb=tb, f=f, s2=s2: e.matmul(
                            out=SB["PO"][:, 512 * s2:512 * (s2 + 1)],
                            lhsT=SB["hmid"][:, f, 128 * tb:128 * (tb + 1)],
                            rhs=SB["w2_sb"][:, f, 512 * s2:512 * (s2 + 1)],
                            start=(f == 0), stop=(f == 31)), inc=te,
                            waits=([(ac, mm1_ac), (dv, st["po2_rel"]),
                                    (gd, moe_ld_gd)] if (f == 0 and s2 == 0) else None))
                O(dv, lambda e, tb=tb: e.tensor_add(out=SB["ybsb"][:, :], in0=SB["PO"][:, 0:1024],
                                                    in1=SB["b2_128"][:, :]), inc=dv,
                  waits=[(te, S[te]), (gd, st["ybsb_rel"]), (sy, moe_ld_sy)])
                st["po2_rel"] = S[dv]
                O(gp, lambda e, c=c, kk=kk, tb=tb: e.dma_start(
                    out=ap(DR[f"yin{c}"])[512 * kk + 128 * tb:512 * kk + 128 * (tb + 1), :],
                    in_=SB["ybsb"][:, :]), inc=gd, amt=16, waits=[(dv, S[dv])])
                st["ybsb_rel"] = S[gd]
            st["mm1_last_te"] = S[te]
            st["hmid_rel"] = S[te]
        if f"yin{c}t" in dbg:
            O(gp, lambda e, c=c: e.dma_start(out=dbg[f"yin{c}t"][:, :],
                                             in_=ap(DR[f"yin{c}"])[:, :]),
              inc=gd, amt=16, waits=[(gd, S[gd])])
        O(gp, lambda e, c=c: e.collective_compute(
            "AllGather", ALU.bypass, replica_groups=[list(range(NC_))],
            ins=[ap(DR[f"yin{c}"]).opt()], outs=[ap(DR[f"yout{c}"]).opt()]), inc=cc,
            waits=[(gd, S[gd])])
    ag2_cc = S[cc]
    if "yb0" in dbg:
        O(gp, lambda e: e.dma_start(out=dbg["yb0"][:, :], in_=ap(DR["yout0"])[:, :]),
          inc=gd, amt=16, waits=[(cc, ag2_cc)])
    if "yb1" in dbg:
        O(gp, lambda e: e.dma_start(out=dbg["yb1"][:, :], in_=ap(DR["yout1"])[:, :]),
          inc=gd, amt=16, waits=[(cc, ag2_cc)])

    if MAX_PHASE < 6:
        O(dv, lambda e: e.engine_nop(),
          waits=[(sy, S[sy]), (gd, S[gd]), (cc, S[cc]), (gp, S[gp]), (ac, S[ac]), (te, S[te])])
        return
    em.barrier()

    st["gbuf_rel"] = 0
    st["hbuf_rel"] = 0
    st["ybuf_rel"] = 0
    for kt in range(16):
        for c in range(2):
            for m in range(4):
                O(gp, lambda e, c=c, kt=kt, m=m: e.indirect_dma_start(
                    out=SB[f"gbuf{c}"][:, m, :], out_offset=None,
                    in_=ap(DR[f"yout{c}"])[:, :],
                    in_offset=bass.IndirectOffsetOnAxis(
                        ap=SB[f"yri{c}"][:, 4 * kt + m:4 * kt + m + 1], axis=0)),
                    inc=gd, amt=16,
                    waits=([(cc, ag2_cc), (dv, st["gbuf_rel"])]
                           if (c == 0 and m == 0) else None))
        O(sy, lambda e, kt=kt: e.dma_start(
            out=SB["hbuf"][:, :, :],
            in_=DR["h32"].rearrange("(i p) d -> p i d", p=128)[:, 4 * kt:4 * (kt + 1), :]),
            inc=sy, amt=16, waits=[(dv, st["hbuf_rel"])])
        tile_ld_gd = S[gd]
        tile_ld_sy = S[sy]
        for m in range(4):
            col = 4 * kt + m
            O(dv, lambda e, m=m, col=col: e.scalar_tensor_tensor(
                out=SB["ybuf"][:, :], in0=SB["gbuf0"][:, m, :],
                scalar=SB["scl2"].ap()[:, col, 0:1], in1=SB["hbuf"][:, m, :],
                op0=ALU.mult, op1=ALU.add), inc=dv,
                waits=[(gd, tile_ld_gd), (sy, tile_ld_sy), (sy, st["ybuf_rel"])])
            O(dv, lambda e, m=m, col=col: e.scalar_tensor_tensor(
                out=SB["ybuf"][:, :], in0=SB["gbuf1"][:, m, :],
                scalar=SB["scl2"].ap()[:, col, 1:2], in1=SB["ybuf"][:, :],
                op0=ALU.mult, op1=ALU.add), inc=dv)
            O(dv, lambda e: e.reduce_sum(out=SB["mn"][:, 0:1], in_=SB["ybuf"][:, :], axis=AXX),
              inc=dv)
            O(ac, lambda e: e.activation(out=SB["scr4"][:, :], in_=SB["ybuf"][:, :],
                                         func=AF.Square, accum_out=SB["mn"][:, 1:2]), inc=ac,
              waits=[(dv, S[dv])])
            O(dv, lambda e: e.tensor_scalar(out=SB["mn"][:, 0:1], in0=SB["mn"][:, 0:1],
                                            scalar1=1.0 / D, scalar2=None, op0=ALU.mult), inc=dv,
              waits=[(ac, S[ac])])
            O(dv, lambda e: e.tensor_tensor(out=SB["mn"][:, 2:3], in0=SB["mn"][:, 0:1],
                                            in1=SB["mn"][:, 0:1], op=ALU.mult), inc=dv,
              waits=[(dv, S[dv])])
            O(dv, lambda e: e.tensor_scalar(out=SB["mn"][:, 1:2], in0=SB["mn"][:, 1:2],
                                            scalar1=1.0 / D, scalar2=EPS, op0=ALU.mult,
                                            op1=ALU.add), inc=dv)
            O(dv, lambda e: e.tensor_sub(out=SB["mn"][:, 1:2], in0=SB["mn"][:, 1:2],
                                         in1=SB["mn"][:, 2:3]), inc=dv,
              waits=[(dv, S[dv])])
            O(dv, lambda e: e.reciprocal(out=SB["mn"][:, 3:4], in_=SB["mn"][:, 1:2]), inc=dv,
              waits=[(dv, S[dv])])
            O(ac, lambda e: e.activation(out=SB["mn"][:, 4:5], in_=SB["mn"][:, 3:4],
                                         func=AF.Sqrt), inc=ac, waits=[(dv, S[dv])])
            O(dv, lambda e: e.tensor_scalar(out=SB["scr4"][:, :], in0=SB["ybuf"][:, :],
                                            scalar1=SB["mn"][:, 0:1], scalar2=SB["mn"][:, 4:5],
                                            op0=ALU.subtract, op1=ALU.mult), inc=dv,
              waits=[(ac, S[ac])])
            O(sy, lambda e, col=col: e.dma_start(
                out=DR["out"].rearrange("(i p) d -> p i d", p=128)[:, col:col + 1, :],
                in_=SB["scr4"].ap().unsqueeze(1)), inc=sy, amt=16, waits=[(dv, S[dv])])
            st["ybuf_rel"] = S[sy]
        st["gbuf_rel"] = S[dv]
        st["hbuf_rel"] = S[dv]

    O(dv, lambda e: e.engine_nop(),
      waits=[(sy, S[sy]), (gd, S[gd]), (cc, S[cc]), (gp, S[gp]), (ac, S[ac]), (te, S[te])])


_NC_CACHE = {}


def build_nc(dbg_taps=()):
    key = (tuple(dbg_taps), MAX_PHASE, MOE_NOGATHER)
    if key in _NC_CACHE:
        return _NC_CACHE[key]
    nc = bass.Bass("TRN2", target_bir_lowering=False)
    CK = host_consts()
    DR = {}
    DR["x"] = nc.dram_tensor("x", [N, D], F32, kind="ExternalInput").ap()
    DR["wqkv"] = nc.dram_tensor("wqkv", [D, 3 * D], F32, kind="ExternalInput").ap()
    DR["wout"] = nc.dram_tensor("wout", [D, D], F32, kind="ExternalInput").ap()
    DR["qs"] = nc.dram_tensor("qs", [1, DH], F32, kind="ExternalInput").ap()
    DR["ks"] = nc.dram_tensor("ks", [1, DH], F32, kind="ExternalInput").ap()
    DR["gwT"] = nc.dram_tensor("gwT", [D, E], F32, kind="ExternalInput").ap()
    DR["gb"] = nc.dram_tensor("gb", [1, E], F32, kind="ExternalInput").ap()
    DR["w1"] = nc.dram_tensor("w1", [D, FF], F32, kind="ExternalInput").ap()
    DR["b1"] = nc.dram_tensor("b1", [1, FF], F32, kind="ExternalInput").ap()
    DR["w2"] = nc.dram_tensor("w2", [FF, D], F32, kind="ExternalInput").ap()
    DR["b2"] = nc.dram_tensor("b2", [1, D], F32, kind="ExternalInput").ap()
    DR["cconst"] = nc.dram_tensor("cconst", [128, 8], F32, kind="ExternalInput").ap()
    DR["bmask"] = nc.dram_tensor("bmask", [8, 4], F32, kind="ExternalInput").ap()
    DR["esel"] = nc.dram_tensor("esel", [1, 32], F32, kind="ExternalInput").ap()
    DR["out"] = nc.dram_tensor("out", [N, D], F32, kind="ExternalOutput").ap()
    DR["xbf"] = nc.dram_tensor("xbf", [N, D], BF16).ap()
    DR["h32"] = nc.dram_tensor("h32", [N, D], F32).ap()
    DR["hagin"] = nc.dram_tensor("hagin", [N, D], BF16)
    DR["hagout"] = nc.dram_tensor("hagout", [NC_ * N, D], BF16)
    DR["lgin"] = nc.dram_tensor("lgin", [N, E], F32)
    DR["lgout"] = nc.dram_tensor("lgout", [NC_ * N, E], F32)
    DR["rtin"] = nc.dram_tensor("rtin", [4, E * CAP], F16)
    DR["rtout"] = nc.dram_tensor("rtout", [4, E * CAP], F16)
    DR["yin0"] = nc.dram_tensor("yin0", [CAP + 1, D], BF16)
    DR["yin1"] = nc.dram_tensor("yin1", [CAP + 1, D], BF16)
    DR["yout0"] = nc.dram_tensor("yout0", [NC_ * (CAP + 1), D], BF16)
    DR["yout1"] = nc.dram_tensor("yout1", [NC_ * (CAP + 1), D], BF16)
    DR["w1bf"] = nc.dram_tensor("w1bf", [D, FF], BF16).ap()
    DR["w1bfT"] = DR["w1bf"]
    DR["sc2"] = nc.dram_tensor("sc2", [64, 16], F32).ap()
    DR["sc3"] = nc.dram_tensor("sc3", [1, 64], F32).ap()
    DR["IC"] = {k: nc.inline_tensor(np.ascontiguousarray(v), name="c_" + k)
                for k, v in CK.items()}
    DR["nc"] = nc

    dbg = {}
    for nm in dbg_taps:
        shp, dt = {"h32": ([N, D], F32), "lgout": ([NC_ * N, E], F32),
                   "hagr": ([3072, D], BF16), "lgin0": ([N, E], F32),
                   "rtout": ([4, E * CAP], F16),
                   "yb0": ([NC_ * (CAP + 1), D], BF16),
                   "yb1": ([NC_ * (CAP + 1), D], BF16),
                   "xbm10": ([128, 8, 512], BF16), "xbA10": ([128, 8, 512], BF16),
                   "xbB10": ([128, 8, 512], BF16),
                   "yin1t": ([CAP + 1, D], BF16), "yin0t": ([CAP + 1, D], BF16),
                   "qhat0": ([128, D], F16), "khat0": ([128, D], F16),
                   "ptr0": ([128, H, 384], BF16), "vr0": ([128, H, 65], BF16),
                   "att0": ([128, D], BF16), "rs0": ([128, H], F32),
                   "y10": ([128, D], F32), "qtmp0": ([128, D], F32),
                   "gr20": ([128, 64, 2], F32), "eown0": ([128, 64, 2], F32),
                   "ohown0": ([128, 64, 2, E], F32), "eownX": ([128, 64, 2], F32),
                   "ohg0": ([128, 512, 2, E], F32), "lgt0": ([128, 512, E], F32),
                   "cnti": ([1, 32], I32), "off1280": ([128, 8], F32),
                   "xidx00t": ([128, 160], I16), "xidx01t": ([128, 160], I16),
                   "xidx10t": ([128, 160], I16), "xidx11t": ([128, 160], I16),
                   "yri0t": ([128, 64], I32), "yri1t": ([128, 64], I32),
                   "scl20": ([128, 64, 2], F32),
                   "rsml0": ([1, 128], F32), "counts2": ([8, 16], F32),
                   "lgown0": ([128, 64, E], F32),
                   "cnts0": ([8, 16], F32), "tots0": ([65, 16], F32),
                   "keep0": ([128, 64, 2], F32), "pown0": ([128, 64, 2], F32)}[nm]
        dbg[nm] = nc.dram_tensor("dbg_" + nm, shp, dt, kind="ExternalOutput").ap()

    SB = sbuf_map(nc)
    import contextlib
    with contextlib.ExitStack() as ctx:
        SB["PA"] = ctx.enter_context(nc.psum_tensor("PA", [128, 512], F32))
        SB["PB"] = ctx.enter_context(nc.psum_tensor("PB", [128, 512], F32))
        SB["PSIM"] = ctx.enter_context(nc.psum_tensor("PSIM", [128, 4, 384], F32))
        SB["PO"] = ctx.enter_context(nc.psum_tensor("PO", [128, 1536], F32))
        with contextlib.ExitStack() as sctx:
            sems = {}
            for name in ("te", "dv", "ac", "gp", "sy", "gd", "cc"):
                sems[name] = sctx.enter_context(nc.semaphore("sem_" + name))
            with nc.Block() as block:
                @block.tensor
                def _(e):
                    emit_program(Emit("te", e, sems), dbg, DR, SB)

                @block.vector
                def _(e):
                    emit_program(Emit("dv", e, sems), dbg, DR, SB)

                @block.scalar
                def _(e):
                    emit_program(Emit("ac", e, sems), dbg, DR, SB)

                @block.gpsimd
                def _(e):
                    emit_program(Emit("gp", e, sems), dbg, DR, SB)

                @block.sync
                def _(e):
                    emit_program(Emit("sy", e, sems), dbg, DR, SB)

    mybir.codegen_inst_isa_subclasses(nc)
    _NC_CACHE[key] = nc
    return nc


def kernel(**inputs):
    X = np.ascontiguousarray(np.asarray(inputs["L0_attn_x"], dtype=np.float32))
    wqkv = np.ascontiguousarray(np.asarray(inputs["L0_attn_w_qkv"], dtype=np.float32))
    wout = np.ascontiguousarray(np.asarray(inputs["L0_attn_w_out"], dtype=np.float32))
    qs = np.ascontiguousarray(np.asarray(inputs["L0_attn_q_scale"], dtype=np.float32)).reshape(1, DH)
    ks = np.ascontiguousarray(np.asarray(inputs["L0_attn_k_scale"], dtype=np.float32)).reshape(1, DH)
    gw = np.asarray(inputs["L0_ffn_gate_w"], dtype=np.float32)
    gwT = np.ascontiguousarray(gw.T)
    gb = np.ascontiguousarray(np.asarray(inputs["L0_ffn_gate_b"], dtype=np.float32)).reshape(1, E)
    W1 = np.asarray(inputs["L0_ffn_W1"], dtype=np.float32)
    B1 = np.asarray(inputs["L0_ffn_b1"], dtype=np.float32)
    W2 = np.asarray(inputs["L0_ffn_W2"], dtype=np.float32)
    B2 = np.asarray(inputs["L0_ffn_b2"], dtype=np.float32)

    nc = build_nc(tuple(DBG_TAPS))
    in_maps = []
    for b in range(NC_):
        cconst = np.zeros((128, 8), dtype=np.float32)
        cconst[:, 0] = b
        cconst[:, 1] = 4 * b
        cconst[:, 2] = 8192 * b + 1
        bmask = np.zeros((8, 4), dtype=np.float32)
        bmask[:b, 0] = 1.0
        bmask[:4, 1] = 1.0
        bmask[:, 2] = 1.0
        esel1 = np.zeros(8, dtype=np.float32)
        esel1[b] = 1.0
        esel = np.ascontiguousarray(np.tile(esel1, 4)[None])
        in_maps.append({
            "x": np.ascontiguousarray(X[b]),
            "wqkv": wqkv, "wout": wout, "qs": qs, "ks": ks,
            "gwT": gwT, "gb": gb,
            "w1": np.ascontiguousarray(W1[b]),
            "b1": np.ascontiguousarray(B1[b]).reshape(1, FF),
            "w2": np.ascontiguousarray(W2[b]),
            "b2": np.ascontiguousarray(B2[b]).reshape(1, D),
            "cconst": cconst, "bmask": bmask, "esel": esel,
        })
    res = run_bass_kernel_spmd(nc, in_maps, core_ids=list(range(NC_)))
    out = np.stack([res.results[b]["out"] for b in range(NC_)], axis=0)
    if DBG_TAPS:
        kernel.dbg = [{k: res.results[b]["dbg_" + k] for k in DBG_TAPS} for b in range(NC_)]
    return out
